# version 16
# baseline (speedup 1.0000x reference)
import contextlib

import numpy as np

import concourse.bass as bass
import concourse.mybir as mybir
from concourse.bass_utils import run_bass_kernel_spmd

F32 = mybir.dt.float32
BF16 = mybir.dt.bfloat16
I32 = mybir.dt.int32
AF = mybir.ActivationFunctionType
ALU = mybir.AluOpType
AX = mybir.AxisListType

N_CORES = 8
EPS = 1e-5


class Chan:
    def __init__(self, handle):
        self.h = handle
        self.n = 0


class Plan:
    ENGS = ("sync", "gpsimd", "tensor", "vector", "scalar")

    def __init__(self, nc, stack, tag):
        self.nc = nc
        self.stack = stack
        self.tag = tag
        self.streams = {e: [] for e in self.ENGS}
        self.ch = {e: Chan(stack.enter_context(nc.semaphore(f"{tag}_{e}")))
                   for e in self.ENGS}

    def dchan(self, name):
        if name not in self.ch:
            self.ch[name] = Chan(
                self.stack.enter_context(self.nc.semaphore(f"{self.tag}_{name}")))
        return self.ch[name]

    def wait(self, eng, chan, val):
        if val <= 0:
            return
        self.streams[eng].append(("w", chan.h, val))

    def op(self, eng, meth, *args, inc=False, inc_n=1, chan=None, **kw):
        ch = self.ch[chan] if chan else self.ch[eng]
        upd = None
        if inc:
            ch.n += inc_n
            upd = (ch.h, inc_n)
        self.streams[eng].append(("o", meth, args, kw, upd))
        return ch.n

    def dma(self, eng, name, dst, src, **kw):
        c = self.dchan(name)
        self.op(eng, "dma_start", dst, src, inc=True, inc_n=16, chan=name, **kw)
        return (c, c.n)

    def emit(self):
        for name, ch in self.ch.items():
            if name not in self.ENGS and ch.n:
                self.streams["sync"].append(("w", ch.h, ch.n))

        def runner(stream):
            def run(e):
                for it in stream:
                    if it[0] == "w":
                        e.wait_ge(it[1], it[2])
                    else:
                        _, meth, args, kw, upd = it
                        ins = getattr(e, meth)(*args, **kw)
                        if upd is not None:
                            ins.then_inc(upd[0], upd[1])
            return run

        with self.nc.allow_low_precision(reason="bf16 matmul operand staging"), \
                self.nc.Block() as blk:
            blk.sync(runner(self.streams["sync"]))
            blk.gpsimd(runner(self.streams["gpsimd"]))
            blk.tensor(runner(self.streams["tensor"]))
            blk.vector(runner(self.streams["vector"]))
            blk.scalar(runner(self.streams["scalar"]))


def plan_ln_chain(p, cst, src_ap, dst_ap, F, st, sq, prev_gate, sq_state=None):
    VE, AE = p.ch["vector"], p.ch["scalar"]
    if prev_gate:
        p.wait("vector", VE, prev_gate)
    s1 = p.op("vector", "reduce_sum", st[:, 0:1], src_ap, AX.X, inc=True)
    if sq_state is not None and sq_state.get("n"):
        p.wait("vector", VE, sq_state["n"])
    s2 = p.op("vector", "scalar_tensor_tensor", sq[:, 0:F], src_ap, 1.0,
              src_ap, ALU.bypass, ALU.mult, accum_out=st[:, 1:2], inc=True)
    if sq_state is not None:
        sq_state["n"] = s2
    p.wait("vector", VE, s1)
    m = p.op("vector", "tensor_scalar_mul", st[:, 2:3], st[:, 0:1], 1.0 / F,
             inc=True)
    p.wait("vector", VE, m)
    m2 = p.op("vector", "tensor_mul", st[:, 3:4], st[:, 2:3], st[:, 2:3],
              inc=True)
    p.wait("vector", VE, m2)
    var = p.op("vector", "scalar_tensor_tensor", st[:, 4:5], st[:, 1:2], 1.0 / F,
               st[:, 3:4], ALU.mult, ALU.subtract, inc=True)
    p.wait("scalar", VE, var)
    ln_ = p.op("scalar", "activation", st[:, 5:6], st[:, 4:5], AF.Ln,
               bias=cst[:, 0:1], inc=True)
    p.wait("scalar", AE, ln_)
    rstd = p.op("scalar", "activation", st[:, 6:7], st[:, 5:6], AF.Exp,
                scale=cst[:, 1:2], inc=True)
    p.wait("vector", AE, rstd)
    return p.op("vector", "tensor_scalar", dst_ap, src_ap, st[:, 2:3],
                st[:, 6:7], ALU.subtract, ALU.mult, inc=True)


def build_program(T=4096, D=1024, HID=4096, NH=16, HD=64):
    assert D % 512 == 0 and T % 512 == 0 and HID % 512 == 0 and NH * HD == D
    assert HD == 64 and NH % 2 == 0
    DC = D // 128
    NG = T // 512
    HC = HID // 128
    NCK = T // 128
    NPR = NCK // 2
    NTT = T // 128
    VW = 66

    nc = bass.Bass("TRN2", target_bir_lowering=False)

    x_h = nc.dram_tensor("x", [T, D], F32, kind="ExternalInput").ap()
    Wq_h = nc.dram_tensor("Wq", [D, D], F32, kind="ExternalInput").ap()
    Wk_h = nc.dram_tensor("Wk", [D, D], F32, kind="ExternalInput").ap()
    Wv_h = nc.dram_tensor("Wv", [D, D], F32, kind="ExternalInput").ap()
    Wo_h = nc.dram_tensor("Wo", [D, D], F32, kind="ExternalInput").ap()
    bo_h = nc.dram_tensor("bo", [D], F32, kind="ExternalInput").ap()
    out_h = nc.dram_tensor("out", [T, D], F32, kind="ExternalOutput").ap()

    W_SPECS = [("w1", HID, D), ("w2", HID, D), ("h1", HID, HID),
               ("h2", HID, HID), ("w3", D, HID)]
    W_NAT, W_SCL = {}, {}
    for nm, No, Ni in W_SPECS:
        W_NAT[nm] = nc.dram_tensor(nm + "w", [No, Ni], F32, kind="ExternalInput").ap()
        W_SCL[nm] = nc.dram_tensor(nm + "s", [No], F32, kind="ExternalInput").ap()

    ln1_s = nc.dram_tensor("ln1_s", [NG, 128, DC * 512], BF16, kind="Internal").ap()
    y_s = nc.dram_tensor("y_s", [T, D], F32, kind="Internal").ap()
    wt_s = {nm: nc.dram_tensor(nm + "Ts", [No // 256, 128, (Ni // 128) * 256],
                               BF16, kind="Internal").ap()
            for nm, No, Ni in W_SPECS}

    with contextlib.ExitStack() as top:
        ident = top.enter_context(nc.sbuf_tensor("ident", [128, 128], BF16))
        ones_bf = top.enter_context(nc.sbuf_tensor("ones_bf", [1, 128], BF16))
        bo_sb = top.enter_context(nc.sbuf_tensor("bo_sb", [128, D], F32))
        cst = top.enter_context(nc.sbuf_tensor("cst", [128, 4], F32))

        with contextlib.ExitStack() as ps:
            WMAX = max(D, HID)
            natf = [ps.enter_context(nc.sbuf_tensor(f"natf{i}", [128, WMAX], F32))
                    for i in range(3)]
            scb = [ps.enter_context(nc.sbuf_tensor(f"scb{i}", [128, WMAX], BF16))
                   for i in range(3)]
            stage = [ps.enter_context(nc.sbuf_tensor(f"stage{i}", [128, HC * 256], BF16))
                     for i in range(2)]
            scol = ps.enter_context(nc.sbuf_tensor("scol", [128, 3], F32))
            iden_i = ps.enter_context(nc.sbuf_tensor("iden_i", [128, 128], I32))
            ones_f = ps.enter_context(nc.sbuf_tensor("ones_f", [1, 128], F32))
            borow = ps.enter_context(nc.sbuf_tensor("borow", [1, D], F32))
            ptr = [ps.enter_context(nc.psum_tensor(f"ptr{i}", [128, 512], BF16))
                   for i in range(2)]
            pbo = ps.enter_context(nc.psum_tensor("pbo", [128, 512], F32))

            p = Plan(nc, ps, "P")
            GP, TE, VE, AE = p.ch["gpsimd"], p.ch["tensor"], p.ch["vector"], \
                p.ch["scalar"]

            io = p.op("gpsimd", "iota", iden_i[:, :], [[1, 128]], base=0,
                      channel_multiplier=-1, inc=True)
            p.wait("gpsimd", GP, io)
            p.op("gpsimd", "tensor_single_scalar", ident[:, :], iden_i[:, :], 0,
                 ALU.is_equal)
            p.op("gpsimd", "memset", ones_bf[:, :], 1.0)
            p.op("gpsimd", "memset", cst[:, 0:1], EPS)
            p.op("gpsimd", "memset", cst[:, 1:2], -0.5)
            p.op("gpsimd", "memset", cst[:, 2:3], 0.125)
            consts_done = p.op("gpsimd", "memset", ones_f[:, :], 1.0, inc=True)

            bo_ld = p.dma("sync", "d_bo", borow[:, :], bo_h[None, :])
            bo_ev = 0
            for half in range(D // 512):
                p.wait("tensor", *bo_ld)
                p.wait("tensor", GP, consts_done)
                if bo_ev:
                    p.wait("tensor", VE, bo_ev)
                bo_mm = p.op("tensor", "matmul", pbo[:, :], ones_f[:, :],
                             borow[:, half * 512:(half + 1) * 512],
                             start=True, stop=True, inc=True)
                p.wait("vector", TE, bo_mm)
                bo_ev = p.op("vector", "tensor_copy",
                             bo_sb[:, half * 512:(half + 1) * 512], pbo[:, :],
                             inc=True)

            natf_free = [0, 0, 0]
            scb_free = [0, 0, 0]
            ptr_free = [0, 0]
            stage_free = [None, None]
            oc_glob = 0
            trb_ctr = 0
            for nm, No, Ni in W_SPECS:
                nic = Ni // 128
                ntrb = (nic + 3) // 4
                for oc in range(No // 128):
                    par = oc_glob % 3
                    og = oc // 2
                    if natf_free[par]:
                        p.wait("sync", VE, natf_free[par])
                    sc_ld = p.dma("sync", f"d_scol{par}", scol[:, par:par + 1],
                                  W_SCL[nm][oc * 128:(oc + 1) * 128][:, None],
                                  allow_slow_non_contiguous=True)
                    nat_ld = p.dma("sync", f"d_natf{par}", natf[par][:, 0:Ni],
                                   W_NAT[nm][oc * 128:(oc + 1) * 128, :])
                    p.wait("vector", *sc_ld)
                    p.wait("vector", *nat_ld)
                    if scb_free[par]:
                        p.wait("vector", TE, scb_free[par])
                    scmul = p.op("vector", "tensor_scalar_mul", scb[par][:, 0:Ni],
                                 natf[par][:, 0:Ni], scol[:, par:par + 1],
                                 inc=True)
                    natf_free[par] = scmul
                    for b in range(ntrb):
                        nb = min(4, nic - 4 * b)
                        tpar = trb_ctr % 2
                        trb_ctr += 1
                        p.wait("tensor", VE, scmul)
                        if ptr_free[tpar]:
                            p.wait("tensor", AE, ptr_free[tpar])
                        for j in range(nb):
                            trb = p.op("tensor", "transpose",
                                       ptr[tpar][:, j * 128:(j + 1) * 128],
                                       scb[par][:, (4 * b + j) * 128:(4 * b + j + 1) * 128],
                                       ident[:, :], inc=(j == nb - 1))
                        spar = og % 2
                        p.wait("scalar", TE, trb)
                        if b == 0 and oc % 2 == 0 and stage_free[spar]:
                            p.wait("scalar", *stage_free[spar])
                        off = (oc % 2) * 128
                        if nb > 1:
                            dst = stage[spar][:, (4 * b) * 256:(4 * b + nb) * 256] \
                                .rearrange("q (j c) -> q j c", c=256)[:, :, off:off + 128]
                            src = ptr[tpar][:, 0:nb * 128] \
                                .rearrange("q (j c) -> q j c", c=128)
                        else:
                            dst = stage[spar][:, (4 * b) * 256 + off:
                                              (4 * b) * 256 + off + 128]
                            src = ptr[tpar][:, 0:128]
                        stev = p.op("scalar", "copy", dst, src, inc=True)
                        ptr_free[tpar] = stev
                    scb_free[par] = trb
                    if oc % 2 == 1:
                        p.wait("gpsimd", AE, stev)
                        stage_free[og % 2] = p.dma(
                            "gpsimd", f"d_og{og % 2}", wt_s[nm][og],
                            stage[og % 2][:, 0:nic * 256])
                    oc_glob += 1
            p.emit()

        with contextlib.ExitStack() as ks:
            k_sb = ks.enter_context(nc.sbuf_tensor("k_sb", [128, DC * T], BF16))
            v_sb = ks.enter_context(nc.sbuf_tensor("v_sb", [128, NTT * NH * VW], BF16))

            with contextlib.ExitStack() as a1:
                Wk_sb = a1.enter_context(nc.sbuf_tensor("Wk_sb", [128, DC * D], BF16))
                Wv_sb = a1.enter_context(nc.sbuf_tensor("Wv_sb", [128, DC * D], BF16))
                xb = [a1.enter_context(nc.sbuf_tensor(f"a1x{i}", [128, D], F32))
                      for i in range(2)]
                lntm = [a1.enter_context(nc.sbuf_tensor(f"a1l{i}", [128, D], BF16))
                        for i in range(2)]
                sqs = a1.enter_context(nc.sbuf_tensor("a1q", [128, D], BF16))
                stat = [a1.enter_context(nc.sbuf_tensor(f"a1s{i}", [128, 8], F32))
                        for i in range(2)]
                sq_state = {"n": 0}
                ln1g = a1.enter_context(nc.sbuf_tensor("ln1g", [128, DC * 512], BF16))
                KP = a1.enter_context(nc.psum_tensor("KP", [128, 512], F32))
                VP = [a1.enter_context(nc.psum_tensor(f"VP{i}", [128, 512], F32))
                      for i in range(2)]
                ltr = [a1.enter_context(nc.psum_tensor(f"ltr{i}", [128, 512], BF16))
                       for i in range(2)]

                p = Plan(nc, a1, "A1")
                GP, TE, VE, AE = (p.ch[e] for e in ("gpsimd", "tensor", "vector",
                                                    "scalar"))

                for tt in range(NTT):
                    base = tt * NH * VW
                    tile = v_sb[:, base:base + NH * VW] \
                        .rearrange("q (h c) -> q h c", c=VW)
                    p.op("gpsimd", "memset", tile[:, :, 64:65], 1.0)
                    p.op("gpsimd", "memset", tile[:, :, 65:66], 0.0)

                wcast = 0
                for wsb, wh in ((Wk_sb, Wk_h), (Wv_sb, Wv_h)):
                    for ic in range(DC):
                        par = wcast % 2
                        if wcast >= 2:
                            p.wait("sync", VE, wcast - 1)
                        wld = p.dma("sync", f"d_x{par}", xb[par][:, :],
                                    wh[ic * 128:(ic + 1) * 128, :])
                        p.wait("vector", *wld)
                        p.op("vector", "tensor_copy", wsb[:, ic * D:(ic + 1) * D],
                             xb[par][:, :], inc=True)
                        wcast += 1

                xb_free = [wcast - 1, wcast]
                ln_prev = [0, 0]
                ltr_free = [0, 0]
                vp_free = [0, 0]
                kp_free = 0
                l1w = None
                for g in range(NG):
                    for st_i in range(4):
                        u = g * 4 + st_i
                        par = u % 2
                        if xb_free[par]:
                            p.wait("sync", VE, xb_free[par])
                        xld = p.dma("sync", f"d_x{par}", xb[par][:, :],
                                    x_h[u * 128:(u + 1) * 128, :])
                        p.wait("vector", *xld)
                        lnv = plan_ln_chain(p, cst, xb[par][:, :],
                                            lntm[par][:, 0:D], D, stat[par],
                                            sqs, ln_prev[par],
                                            sq_state=sq_state)
                        ln_prev[par] = lnv
                        xb_free[par] = lnv
                        for b in range(DC // 4):
                            tpar = (u * (DC // 4) + b) % 2
                            p.wait("tensor", VE, lnv)
                            if ltr_free[tpar]:
                                p.wait("tensor", VE, ltr_free[tpar])
                            for j in range(4):
                                trb = p.op("tensor", "transpose",
                                           ltr[tpar][:, j * 128:(j + 1) * 128],
                                           lntm[par][:, (4 * b + j) * 128:(4 * b + j + 1) * 128],
                                           ident[:, :], inc=(j == 3))
                            p.wait("vector", TE, trb)
                            if g > 0 and st_i == 0 and b == 0:
                                p.wait("vector", *l1w)
                            dst = ln1g[:, 0:DC * 512] \
                                .rearrange("q (j c) -> q j c", c=512) \
                                [:, 4 * b:4 * b + 4, st_i * 128:st_i * 128 + 128]
                            src = ltr[tpar][:, 0:512] \
                                .rearrange("q (j c) -> q j c", c=128)
                            ev = p.op("vector", "tensor_copy", dst, src, inc=True)
                            ltr_free[tpar] = ev
                    ln1g_done = ev
                    p.wait("sync", VE, ln1g_done)
                    l1w = p.dma("sync", "d_l1w", ln1_s[g], ln1g[:, :])
                    for oc in range(DC):
                        p.wait("tensor", VE, ln1g_done)
                        if kp_free:
                            p.wait("tensor", VE, kp_free)
                        for ic in range(DC):
                            kmm = p.op("tensor", "matmul", KP[:, :],
                                       Wk_sb[:, ic * D + oc * 128:ic * D + oc * 128 + 128],
                                       ln1g[:, ic * 512:(ic + 1) * 512],
                                       start=(ic == 0), stop=(ic == DC - 1),
                                       inc=(ic == DC - 1))
                        p.wait("vector", TE, kmm)
                        kp_free = p.op("vector", "tensor_copy",
                                       k_sb[:, oc * T + g * 512:oc * T + g * 512 + 512],
                                       KP[:, :], inc=True)
                    for st_i in range(4):
                        tt = g * 4 + st_i
                        for oh in range(2):
                            vidx = g * 8 + st_i * 2 + oh
                            vpar = vidx % 2
                            p.wait("tensor", VE, ln1g_done)
                            if vp_free[vpar]:
                                p.wait("tensor", VE, vp_free[vpar])
                            for ic in range(DC):
                                vmm = p.op("tensor", "matmul", VP[vpar][:, :],
                                           ln1g[:, ic * 512 + st_i * 128:ic * 512 + st_i * 128 + 128],
                                           Wv_sb[:, ic * D + oh * 512:ic * D + oh * 512 + 512],
                                           start=(ic == 0), stop=(ic == DC - 1),
                                           inc=(ic == DC - 1))
                            base = tt * NH * VW + oh * (NH // 2) * VW
                            dst = v_sb[:, base:base + (NH // 2) * VW] \
                                .rearrange("q (h c) -> q h c", c=VW)[:, :, 0:64]
                            src = VP[vpar][:, :].rearrange("q (h c) -> q h c", c=64)
                            p.wait("vector", TE, vmm)
                            vp_free[vpar] = p.op("vector", "tensor_copy", dst, src,
                                                 inc=True)
                p.emit()

            with contextlib.ExitStack() as a2:
                Wqo = a2.enter_context(nc.sbuf_tensor("Wqo", [128, DC * D], BF16))
                wstg = a2.enter_context(nc.sbuf_tensor("wstg", [128, D], F32))
                ln1g = a2.enter_context(nc.sbuf_tensor("ln1g2", [128, DC * 512], BF16))
                q_sb = a2.enter_context(nc.sbuf_tensor("q_sb", [128, DC * 512], BF16))
                ptring = a2.enter_context(nc.sbuf_tensor("ptring", [128, 4 * 1024], BF16))
                ocat = a2.enter_context(nc.sbuf_tensor("ocat", [128, DC * 512], BF16))
                bcs = a2.enter_context(nc.sbuf_tensor("bcs", [64, 512], BF16))
                rcp = a2.enter_context(nc.sbuf_tensor("rcp", [1, 1024], BF16))
                zb = [a2.enter_context(nc.sbuf_tensor(f"a2z{i}", [128, D], F32))
                      for i in range(2)]
                xb = [a2.enter_context(nc.sbuf_tensor(f"a2x{i}", [128, D], F32))
                      for i in range(2)]
                sqs = a2.enter_context(nc.sbuf_tensor("a2q", [128, D], BF16))
                stat = [a2.enter_context(nc.sbuf_tensor(f"a2s{i}", [128, 8], F32))
                        for i in range(2)]
                sq_state = {"n": 0}
                SS = a2.enter_context(nc.psum_tensor("SS", [128, 3072], F32))
                OP = [a2.enter_context(nc.psum_tensor(f"OP{i}", [128, 512], F32))
                      for i in range(2)]

                p = Plan(nc, a2, "A2")
                GP, TE, VE, AE = (p.ch[e] for e in ("gpsimd", "tensor", "vector",
                                                    "scalar"))

                def load_swap(wh, gate):
                    nonlocal wstg_free
                    for ic in range(DC):
                        if ic == 0 and gate is not None:
                            p.wait("sync", gate[0], gate[1])
                        if wstg_free:
                            p.wait("sync", VE, wstg_free)
                        wld = p.dma("sync", "d_wstg", wstg[:, :],
                                    wh[ic * 128:(ic + 1) * 128, :])
                        p.wait("vector", *wld)
                        wstg_free = p.op("vector", "tensor_copy",
                                         Wqo[:, ic * D:(ic + 1) * D], wstg[:, :],
                                         inc=True)
                    return wstg_free

                wstg_free = 0
                l1r = p.dma("sync", "d_l1r", ln1g[:, :], ln1_s[0])
                wq_ready = load_swap(Wq_h, None)

                ln_prev = [0, 0]
                ss_exp = {}
                opair = {}
                ofree = {}
                rc_c = {}
                exp_ring = {}
                xb_free = [0, 0]
                zb_free = [None, None]
                ring_rel = {}
                claim_ctr = [0]

                def claim():
                    idx = claim_ctr[0]
                    claim_ctr[0] = idx + 1
                    if idx >= 3 and ring_rel[idx - 3] is not None:
                        p.wait("tensor", *ring_rel[idx - 3])
                    return idx % 3, idx

                for g in range(NG):
                    for oc in range(DC):
                        w, idx = claim()
                        p.wait("tensor", *l1r)
                        p.wait("tensor", VE, wq_ready)
                        for ic in range(DC):
                            qmm = p.op("tensor", "matmul",
                                       SS[:, w * 1024:w * 1024 + 512],
                                       Wqo[:, ic * D + oc * 128:ic * D + oc * 128 + 128],
                                       ln1g[:, ic * 512:(ic + 1) * 512],
                                       start=(ic == 0), stop=(ic == DC - 1),
                                       inc=(ic == DC - 1))
                        p.wait("vector", TE, qmm)
                        qev = p.op("vector", "tensor_copy",
                                   q_sb[:, oc * 512:(oc + 1) * 512],
                                   SS[:, w * 1024:w * 1024 + 512], inc=True)
                        ring_rel[idx] = (VE, qev)
                    q_done_te = qmm
                    q_done_ve = qev

                    wo_ready = load_swap(Wo_h, (TE, q_done_te))

                    def emit_S(h_i, d):
                        hg = g * NH + h_i
                        P = hg * NPR + d
                        kcol = (h_i // 2) * T
                        qcol = (h_i // 2) * 512
                        prow = 64 * (h_i % 2)
                        w, idx = claim()
                        if h_i <= 1 and d == 0:
                            p.wait("tensor", VE, q_done_ve)
                        for ci in range(2):
                            c = 2 * d + ci
                            smm = p.op("tensor", "matmul",
                                       SS[:, w * 1024 + ci * 512:
                                          w * 1024 + ci * 512 + 512],
                                       k_sb[prow:prow + 64,
                                            kcol + c * 128:kcol + c * 128 + 128],
                                       q_sb[prow:prow + 64, qcol:qcol + 512],
                                       start=True, stop=True, inc=(ci == 1))
                        return smm, w, idx

                    ring_owner = {}

                    def emit_exp(h_i, d, smm, w, idx, rs):
                        hg = g * NH + h_i
                        P = hg * NPR + d
                        p.wait("scalar", TE, smm)
                        if rs in ring_owner:
                            p.wait("scalar", TE, opair[ring_owner[rs]])
                        ring_owner[rs] = P
                        ss_exp[P] = p.op("scalar", "activation",
                                         ptring[:, rs * 1024:(rs + 1) * 1024],
                                         SS[:, w * 1024:w * 1024 + 1024],
                                         AF.Exp, scale=cst[:, 2:3], inc=True)
                        exp_ring[P] = rs
                        ring_rel[idx] = (AE, ss_exp[P])

                    def emit_O(h_i, d):
                        hg = g * NH + h_i
                        P = hg * NPR + d
                        rs = exp_ring[P]
                        p.wait("tensor", AE, ss_exp[P])
                        if d == 0 and hg >= 2:
                            p.wait("tensor", VE, ofree[hg - 2])
                        for ci in range(2):
                            c = 2 * d + ci
                            vb = c * NH * VW + h_i * VW
                            omm = p.op("tensor", "matmul", OP[hg % 2][0:66, :],
                                       v_sb[:, vb:vb + 66],
                                       ptring[:, rs * 1024 + ci * 512:
                                              rs * 1024 + ci * 512 + 512],
                                       start=(c == 0), stop=(c == NCK - 1),
                                       inc=(ci == 1))
                        opair[P] = omm

                    def emit_recip(hg):
                        p.wait("vector", TE, opair[hg * NPR + NPR - 1])
                        rc_c[hg] = p.op("vector", "reciprocal",
                                        rcp[:, (hg % 2) * 512:(hg % 2) * 512 + 512],
                                        OP[hg % 2][64:65, :], inc=True)

                    def emit_bcast_finish(hg):
                        h_i = hg % NH
                        w, idx = claim()
                        p.wait("tensor", VE, rc_c[hg])
                        bmm = p.op("tensor", "matmul",
                                   SS[0:64, w * 1024:w * 1024 + 512],
                                   ones_bf[:, 0:64],
                                   rcp[:, (hg % 2) * 512:(hg % 2) * 512 + 512],
                                   start=True, stop=True, inc=True)
                        p.wait("vector", TE, bmm)
                        if hg >= 1:
                            p.wait("vector", VE, ofree[hg - 1])
                        bc_ev = p.op("vector", "tensor_copy", bcs[:, :],
                                     SS[0:64, w * 1024:w * 1024 + 512], inc=True)
                        ring_rel[idx] = (VE, bc_ev)
                        p.wait("vector", VE, bc_ev)
                        ofree[hg] = p.op("vector", "tensor_mul",
                                         ocat[64 * (h_i % 2):64 * (h_i % 2) + 64,
                                              (h_i // 2) * 512:(h_i // 2) * 512 + 512],
                                         OP[hg % 2][0:64, :], bcs[:, :], inc=True)

                    NDP = (NH // 2) * NPR
                    pending_bc = []
                    rs_ctr = [0]

                    def next_rs():
                        rs_ctr[0] = (rs_ctr[0] + 1) % 4
                        return rs_ctr[0]

                    def o_stage(t):
                        hp_o = t // NPR
                        d_o = t % NPR
                        if pending_bc:
                            for hgd in pending_bc:
                                emit_bcast_finish(hgd)
                            pending_bc.clear()
                        emit_O(2 * hp_o, d_o)
                        emit_O(2 * hp_o + 1, d_o)
                        if d_o == NPR - 1:
                            for hgd in (g * NH + 2 * hp_o, g * NH + 2 * hp_o + 1):
                                emit_recip(hgd)
                                pending_bc.append(hgd)

                    for t in range(NDP):
                        hp = t // NPR
                        d = t % NPR
                        se, we, ie = emit_S(2 * hp, d)
                        so, wo, io = emit_S(2 * hp + 1, d)
                        emit_exp(2 * hp, d, se, we, ie, next_rs())
                        emit_exp(2 * hp + 1, d, so, wo, io, next_rs())
                        if t >= 1:
                            o_stage(t - 1)
                    o_stage(NDP - 1)
                    for hgd in pending_bc:
                        emit_bcast_finish(hgd)
                    pending_bc.clear()

                    heads_done = ofree[g * NH + NH - 1]

                    for k in range(8):
                        st_i, oh = divmod(k, 2)
                        u = g * 4 + st_i
                        par = u % 2
                        if oh == 0:
                            if xb_free[par]:
                                p.wait("sync", VE, xb_free[par])
                            xld = p.dma("sync", f"d_x{par}", xb[par][:, :],
                                        x_h[u * 128:(u + 1) * 128, :])
                        wz, idxz = claim()
                        p.wait("tensor", VE, heads_done)
                        p.wait("tensor", VE, wo_ready)
                        for ic in range(DC):
                            wmm = p.op("tensor", "matmul",
                                       SS[:, wz * 1024:wz * 1024 + 512],
                                       ocat[:, ic * 512 + st_i * 128:
                                            ic * 512 + st_i * 128 + 128],
                                       Wqo[:, ic * D + oh * 512:ic * D + oh * 512 + 512],
                                       start=(ic == 0), stop=(ic == DC - 1),
                                       inc=(ic == DC - 1))
                        p.wait("vector", TE, wmm)
                        if oh == 0 and zb_free[par]:
                            p.wait("vector", *zb_free[par])
                        zev = p.op("vector", "tensor_add",
                                   zb[par][:, oh * 512:oh * 512 + 512],
                                   SS[:, wz * 1024:wz * 1024 + 512],
                                   bo_sb[:, oh * 512:oh * 512 + 512], inc=True)
                        ring_rel[idxz] = (VE, zev)
                        if oh == 1:
                            p.wait("vector", *xld)
                            lnfin = plan_ln_chain(p, cst, zb[par][:, :],
                                                  zb[par][:, :], D, stat[par],
                                                  sqs,
                                                  max(ln_prev[par], zev),
                                                  sq_state=sq_state)
                            ln_prev[par] = lnfin
                            p.wait("vector", VE, lnfin)
                            yv = p.op("vector", "tensor_add", zb[par][:, :],
                                      zb[par][:, :], xb[par][:, :], inc=True)
                            xb_free[par] = yv
                            p.wait("sync", VE, yv)
                            zb_free[par] = p.dma("sync", f"d_y{par}",
                                                 y_s[u * 128:(u + 1) * 128, :],
                                                 zb[par][:, :])
                    wo_done_te = wmm
                    if g + 1 < NG:
                        p.wait("sync", TE, q_done_te)
                        l1r = p.dma("sync", "d_l1r", ln1g[:, :], ln1_s[g + 1])
                        wq_ready = load_swap(Wq_h, (TE, wo_done_te))
                p.emit()

        with contextlib.ExitStack() as fsx:
            slab = [fsx.enter_context(nc.sbuf_tensor(f"slab{i}", [128, HC * 256], BF16))
                    for i in range(2)]
            HB = [fsx.enter_context(nc.sbuf_tensor(f"HB{i}", [128, HC * 512], BF16))
                  for i in range(2)]
            g_sb = fsx.enter_context(nc.sbuf_tensor("g_sb", [128, DC * 512], BF16))
            ln2g = g_sb
            f_sb = g_sb
            ftm32 = [fsx.enter_context(nc.sbuf_tensor(f"ftm32_{i}", [128, D], F32))
                     for i in range(4)]
            yb = [fsx.enter_context(nc.sbuf_tensor(f"yb{i}", [128, D], F32))
                  for i in range(4)]
            ybt = [fsx.enter_context(nc.sbuf_tensor(f"ybt{i}", [128, D], F32))
                   for i in range(2)]
            ob = [fsx.enter_context(nc.sbuf_tensor(f"ob{i}", [128, D], F32))
                  for i in range(2)]
            l2tm = [fsx.enter_context(nc.sbuf_tensor(f"l2tm{i}", [128, D], BF16))
                    for i in range(4)]
            sqs = fsx.enter_context(nc.sbuf_tensor("fq", [128, D], BF16))
            stat = [fsx.enter_context(nc.sbuf_tensor(f"fs{i}", [128, 8], F32))
                    for i in range(2)]
            sq_state = {"n": 0}
            PQ = [fsx.enter_context(nc.psum_tensor(f"PQ{i}", [128, 1024], F32))
                  for i in range(2)]
            ftr = [fsx.enter_context(nc.psum_tensor(f"ftr{i}", [128, 512], BF16))
                   for i in range(2)]
            PW = [fsx.enter_context(nc.psum_tensor(f"PW{i}", [128, 512], F32))
                  for i in range(2)]

            p = Plan(nc, fsx, "F")
            GP, TE, VE, AE = (p.ch[e] for e in ("gpsimd", "tensor", "vector",
                                                "scalar"))

            ln_prev = [0, 0]
            slab_free = [0, 0]
            pq_free = [None, None]
            ltr_free = [0, 0]
            pw_free = [0, 0]
            yb_free = [0] * 4
            ybt_free = [0, 0]
            ob_free = [None, None]
            slab_idx = 0

            def ln2_ln_part(f):
                lnvs = []
                for st_i in range(4):
                    u = f * 4 + st_i
                    if yb_free[st_i]:
                        p.wait("sync", VE, yb_free[st_i])
                    yld = p.dma("sync", f"d_y{st_i}", yb[st_i][:, :],
                                y_s[u * 128:(u + 1) * 128, :])
                    p.wait("vector", *yld)
                    if tr_gate[0]:
                        p.wait("vector", TE, tr_gate[0])
                    lnv = plan_ln_chain(p, cst, yb[st_i][:, :],
                                        l2tm[st_i][:, 0:D],
                                        D, stat[st_i % 2], sqs,
                                        max(ln_prev[st_i % 2], 0),
                                        sq_state=sq_state)
                    ln_prev[st_i % 2] = lnv
                    yb_free[st_i] = lnv
                    lnvs.append(lnv)
                return lnvs

            def ln2_tr_part(f, lnvs):
                ev = 0
                for st_i in range(4):
                    for b in range(DC // 4):
                        tpar = (st_i * (DC // 4) + b) % 2
                        p.wait("tensor", VE, lnvs[st_i])
                        if ltr_free[tpar]:
                            p.wait("tensor", VE, ltr_free[tpar])
                        for j in range(4):
                            trb = p.op("tensor", "transpose",
                                       ftr[tpar][:, j * 128:(j + 1) * 128],
                                       l2tm[st_i][:, (4 * b + j) * 128:(4 * b + j + 1) * 128],
                                       ident[:, :], inc=(j == 3))
                        p.wait("vector", TE, trb)
                        dst = ln2g[:, 0:DC * 512] \
                            .rearrange("q (j c) -> q j c", c=512) \
                            [:, 4 * b:4 * b + 4, st_i * 128:st_i * 128 + 128]
                        src = ftr[tpar][:, 0:512].rearrange("q (j c) -> q j c", c=128)
                        ev = p.op("vector", "tensor_copy", dst, src, inc=True)
                        ltr_free[tpar] = ev
                tr_gate[0] = trb
                return ev

            tr_gate = [0]
            lnvs0 = ln2_ln_part(0)
            ln2_ready = ln2_tr_part(0, lnvs0)
            for g in range(NG):

                def stage(nm, nog, nic, rhs_fn, rhs_gate, evac):
                    nonlocal slab_idx
                    for og in range(nog):
                        spar = slab_idx % 2
                        if slab_free[spar]:
                            p.wait("gpsimd", TE, slab_free[spar])
                        sld = p.dma("gpsimd", f"d_slab{spar}",
                                    slab[spar][:, 0:nic * 256], wt_s[nm][og])
                        ppar = slab_idx % 2
                        p.wait("tensor", *sld)
                        if rhs_gate is not None:
                            p.wait("tensor", rhs_gate[0], rhs_gate[1])
                        if pq_free[ppar] is not None:
                            p.wait("tensor", *pq_free[ppar])
                        for oc in range(2):
                            for ic in range(nic):
                                mm = p.op("tensor", "matmul",
                                          PQ[ppar][:, oc * 512:oc * 512 + 512],
                                          slab[spar][:, ic * 256 + oc * 128:
                                                     ic * 256 + oc * 128 + 128],
                                          rhs_fn(ic),
                                          start=(ic == 0), stop=(ic == nic - 1),
                                          inc=(oc == 1 and ic == nic - 1))
                        slab_free[spar] = mm
                        evac(og, ppar, mm)
                        slab_idx += 1

                silu_ev = {}

                def ev_silu(og, ppar, mm):
                    p.wait("scalar", TE, mm)
                    silu_ev[og] = p.op("scalar", "activation",
                                       HB[0][:, og * 1024:og * 1024 + 1024],
                                       PQ[ppar][:, :], AF.Silu, inc=True)
                    pq_free[ppar] = (AE, silu_ev[og])

                stage("w1", HID // 256, DC,
                      lambda ic: ln2g[:, ic * 512:(ic + 1) * 512],
                      (VE, ln2_ready), ev_silu)

                a_ev = 0

                def ev_u2(og, ppar, mm):
                    nonlocal a_ev
                    p.wait("vector", TE, mm)
                    p.wait("vector", AE, silu_ev[og])
                    a_ev = p.op("vector", "tensor_mul",
                                HB[1][:, og * 1024:og * 1024 + 1024],
                                HB[0][:, og * 1024:og * 1024 + 1024],
                                PQ[ppar][:, :], inc=True)
                    pq_free[ppar] = (VE, a_ev)

                stage("w2", HID // 256, DC,
                      lambda ic: ln2g[:, ic * 512:(ic + 1) * 512],
                      (VE, ln2_ready), ev_u2)
                a_done = a_ev
                if g + 1 < NG:
                    lnvs_next = ln2_ln_part(g + 1)
                else:
                    lnvs_next = None

                ge_ev = 0

                def ev_gelu0(og, ppar, mm):
                    nonlocal ge_ev
                    p.wait("scalar", TE, mm)
                    ge_ev = p.op("scalar", "activation",
                                 HB[0][:, og * 1024:og * 1024 + 1024],
                                 PQ[ppar][:, :], AF.Gelu, inc=True)
                    pq_free[ppar] = (AE, ge_ev)

                stage("h1", HID // 256, HC,
                      lambda ic: HB[1][:, ic * 512:(ic + 1) * 512],
                      (VE, a_done), ev_gelu0)
                b_done = ge_ev

                def ev_gelu1(og, ppar, mm):
                    nonlocal ge_ev
                    p.wait("scalar", TE, mm)
                    ge_ev = p.op("scalar", "activation",
                                 HB[1][:, og * 1024:og * 1024 + 1024],
                                 PQ[ppar][:, :], AF.Gelu, inc=True)
                    pq_free[ppar] = (AE, ge_ev)

                stage("h2", HID // 256, HC,
                      lambda ic: HB[0][:, ic * 512:(ic + 1) * 512],
                      (AE, b_done), ev_gelu1)
                c_done = ge_ev
                if lnvs_next is not None:
                    ln2_ready_next = ln2_tr_part(g + 1, lnvs_next)
                else:
                    ln2_ready_next = 0

                fchain = 0
                for og in range(D // 256):
                    spar = slab_idx % 2
                    if slab_free[spar]:
                        p.wait("gpsimd", TE, slab_free[spar])
                    sld = p.dma("gpsimd", f"d_slab{spar}",
                                slab[spar][:, 0:HC * 256], wt_s["w3"][og])
                    for st_i in range(4):
                        k = og * 4 + st_i
                        pwp = k % 2
                        if st_i == 0:
                            p.wait("tensor", *sld)
                            p.wait("tensor", AE, c_done)
                        if pw_free[pwp]:
                            p.wait("tensor", VE, pw_free[pwp])
                        for ic in range(HC):
                            mm = p.op("tensor", "matmul",
                                      PW[pwp][:, (og % 2) * 256:(og % 2) * 256 + 256],
                                      HB[1][:, ic * 512 + st_i * 128:
                                           ic * 512 + st_i * 128 + 128],
                                      slab[spar][:, ic * 256:(ic + 1) * 256],
                                      start=(ic == 0), stop=(ic == HC - 1),
                                      inc=(ic == HC - 1))
                        p.wait("vector", TE, mm)
                        pw_free[pwp] = p.op("vector", "tensor_copy",
                                            ftm32[st_i][:, og * 256:og * 256 + 256],
                                            PW[pwp][:, (og % 2) * 256:(og % 2) * 256 + 256],
                                            inc=True)
                    slab_free[spar] = mm
                    slab_idx += 1
                f_done = pw_free[(D // 256 * 4 - 1) % 2]

                for st_i in range(4):
                    u = g * 4 + st_i
                    par = u % 2
                    if ybt_free[par]:
                        p.wait("sync", VE, ybt_free[par])
                    ytl = p.dma("sync", f"d_yt{par}", ybt[par][:, :],
                                y_s[u * 128:(u + 1) * 128, :])
                    if ob_free[par]:
                        p.wait("vector", *ob_free[par])
                    lnfo = plan_ln_chain(p, cst, ftm32[st_i][:, 0:D], ob[par][:, :],
                                         D, stat[par], sqs,
                                         max(ln_prev[par], f_done),
                                         sq_state=sq_state)
                    ln_prev[par] = lnfo
                    p.wait("vector", *ytl)
                    p.wait("vector", VE, lnfo)
                    ybt_free[par] = p.op("vector", "tensor_add", ob[par][:, :],
                                         ob[par][:, :], ybt[par][:, :], inc=True)
                    p.wait("sync", VE, ybt_free[par])
                    ob_free[par] = p.dma("sync", f"d_out{par}",
                                         out_h[u * 128:(u + 1) * 128, :],
                                         ob[par][:, :])
                ln2_ready = ln2_ready_next
            for par in range(2):
                if ob_free[par]:
                    p.wait("sync", *ob_free[par])
            p.emit()

    return nc


_prog_cache = {}


def _get_program(key):
    if key not in _prog_cache:
        _prog_cache[key] = build_program(*key)
    return _prog_cache[key]


def kernel(**inputs):
    x = np.ascontiguousarray(np.asarray(inputs["L0_attn_x"], dtype=np.float32))
    B, T, D = x.shape
    HID = int(np.asarray(inputs["L0_ffn_w1_w"]).shape[0])
    nc = _get_program((T, D, HID, 16, 64))

    def arr(k):
        return np.ascontiguousarray(np.asarray(inputs[k], dtype=np.float32))

    common = {
        "Wq": arr("L0_attn_Wq"), "Wk": arr("L0_attn_Wk"),
        "Wv": arr("L0_attn_Wv"), "Wo": arr("L0_attn_Wo"),
        "bo": arr("L0_attn_bo"),
        "w1w": arr("L0_ffn_w1_w"), "w1s": arr("L0_ffn_w1_s"),
        "w2w": arr("L0_ffn_w2_w"), "w2s": arr("L0_ffn_w2_s"),
        "h1w": arr("L0_ffn_h1_w"), "h1s": arr("L0_ffn_h1_s"),
        "h2w": arr("L0_ffn_h2_w"), "h2s": arr("L0_ffn_h2_s"),
        "w3w": arr("L0_ffn_w3_w"), "w3s": arr("L0_ffn_w3_s"),
    }
    in_maps = [dict(common, x=np.ascontiguousarray(x[c])) for c in range(B)]
    res = run_bass_kernel_spmd(nc, in_maps, core_ids=list(range(B)))
    return np.stack([r["out"] for r in res.results]).astype(np.float32)
